# version 09
# baseline (speedup 1.0000x reference)
import numpy as np
import ml_dtypes
import concourse.bass as bass
import concourse.mybir as mybir
from concourse.bass_utils import run_bass_kernel_spmd
from contextlib import ExitStack

F32 = mybir.dt.float32
BF16 = mybir.dt.bfloat16
I8 = mybir.dt.int8
AF = mybir.ActivationFunctionType
ALU = mybir.AluOpType
AX = mybir.AxisListType

T = 2048
D = 1024
NL = 6
TT = T // 128
DS = D // 128
EPS = 1e-5
NQS = 8


def build_nc(stop=None, paranoid=0):
    nc = bass.Bass("TRN2", target_bir_lowering=False)
    nc.detect_race_conditions = False

    x_in = nc.dram_tensor("x", [T, D], F32, kind="ExternalInput").ap()
    wqkv = nc.dram_tensor("wqkv", [NL, D, 3 * D], F32, kind="ExternalInput").ap()
    wout = nc.dram_tensor("wout", [NL, D, D], F32, kind="ExternalInput").ap()
    bout = nc.dram_tensor("bout", [NL, D], F32, kind="ExternalInput").ap()
    gamma = nc.dram_tensor("gamma", [NL, D], F32, kind="ExternalInput").ap()
    beta = nc.dram_tensor("beta", [NL, D], F32, kind="ExternalInput").ap()
    fw = nc.dram_tensor("fw", [D, D], F32, kind="ExternalInput").ap()
    fb = nc.dram_tensor("fb", [D], F32, kind="ExternalInput").ap()
    y_out = nc.dram_tensor("y", [T, D], F32, kind="ExternalOutput").ap()

    a_hbm = nc.dram_tensor("a_scr", [T, D], F32, kind="Internal").ap()
    h_hbm = nc.dram_tensor("h_scr", [T, D], BF16, kind="Internal").ap()
    q_hbm = nc.dram_tensor("q_scr", [D, T], BF16, kind="Internal").ap()
    xq_hbm = nc.dram_tensor("xq_scr", [T, D], BF16, kind="Internal").ap()
    wq_hbm = nc.dram_tensor("wq_scr", [D, D], BF16, kind="Internal").ap()
    ws_hbm = nc.dram_tensor("ws_scr", [D], F32, kind="Internal").ap()
    ydbg = qdbg = atdbg = odbg = vdbg = sdbg = None
    if stop is not None:
        ydbg = nc.dram_tensor("ydbg", [T, D], BF16, kind="ExternalOutput").ap()
        qdbg = nc.dram_tensor("qdbg", [D, T], BF16, kind="ExternalOutput").ap()
        atdbg = nc.dram_tensor("atdbg", [128, DS * D], BF16, kind="ExternalOutput").ap()
        odbg = nc.dram_tensor("odbg", [128, TT * D], BF16, kind="ExternalOutput").ap()
        vdbg = nc.dram_tensor("vdbg", [128, TT * D], BF16, kind="ExternalOutput").ap()
        sdbg = nc.dram_tensor("sdbg", [128, 128], F32, kind="ExternalOutput").ap()
        eqall = nc.dram_tensor("eqall", [D, T], F32, kind="ExternalOutput").ap()

    st = ExitStack()
    with st:
        sb = lambda name, shape, dt: st.enter_context(nc.sbuf_tensor(name, shape, dt))
        arena_h = sb("arena_h", [128, DS * T], BF16)
        arena_q = sb("arena_q", [128, TT * D], BF16)
        arena_k = sb("arena_k", [128, TT * D], BF16)
        arena_at = sb("arena_at", [128, DS * D], BF16)
        wbufA = sb("wbufA", [128, DS * D], BF16)
        wbufB = sb("wbufB", [128, DS * D], BF16)
        eq = sb("eq", [128, 2 * T], F32)
        qstage = sb("qstage", [128, 2 * T], BF16)
        ktmp = sb("ktmp", [128, 2 * D], F32)
        aold = sb("aold", [128, 2 * D], F32)
        anew = sb("anew", [128, 2 * D], F32)
        hstage = sb("hstage", [128, 2 * D], BF16)
        i8t = sb("i8t", [128, 2 * D], I8)
        ident = sb("ident", [128, 128], BF16)
        cbuf = sb("cbuf", [128, D], F32)
        bwkrow = sb("bwkrow", [1, D], BF16)
        boutrow = sb("boutrow", [1, D], BF16)
        onesrow = sb("onesrow", [1, 128], BF16)
        bwv = sb("bwv", [128, DS], F32)
        betacol = sb("betacol", [128, DS], BF16)
        gcol = sb("gcol", [128, DS], F32)
        xs = sb("xs", [128, TT], F32)
        stats = sb("stats", [128, 128], F32)
        consts = sb("consts", [128, 4], F32)
        ps = st.enter_context(nc.psum_tensor("ps", [128, 4096], F32))

        sems = {}
        sem_names = (["sT", "sV", "sS", "sG"]
                     + ["gq%d" % i for i in range(NQS)]
                     + ["sq%d" % i for i in range(NQS)])
        for name in sem_names:
            sems[name] = st.enter_context(nc.semaphore(name))
        cnt = {"sT": 0, "sV": 0, "sS": 0, "sG": 0}
        quse = {name: 0 for name in sem_names if name not in cnt}
        qrr = {"gq": 0, "sq": 0}
        seen = {}
        ENG = {"T": nc.tensor, "V": nc.vector, "S": nc.scalar,
               "G": nc.gpsimd, "Y": nc.sync}

        def wt(ek, sk, val):
            if sk is None or val is None or val <= 0:
                return
            key = (ek, sk)
            if seen.get(key, 0) >= val:
                return
            ENG[ek].wait_ge(sems[sk], val)
            seen[key] = val

        def wtt(ek, tok):
            if tok:
                wt(ek, tok[0], tok[1])

        def wtl(ek, toks):
            for tok in toks:
                wtt(ek, tok)

        def inc(ins, sk, n=1):
            ins.then_inc(sems[sk], n)
            cnt[sk] += n
            return cnt[sk]

        def dma(qk, out, in_, waits=(), transpose=False):
            ek = "G" if qk == "gq" else "Y"
            i = qrr[qk]
            qrr[qk] = (i + 1) % NQS
            sname = qk + str(i)
            prev = quse[sname]
            wt(ek, sname, prev)
            for tok in waits:
                wtt(ek, tok)
            if transpose:
                ins = ENG[ek].dma_start_transpose(out, in_)
            else:
                ins = ENG[ek].dma_start(out, in_)
            ins.then_inc(sems[sname], 16)
            quse[sname] = prev + 16
            return (sname, prev + 16)

        bank_w = [0] * 8
        bank_r = [[] for _ in range(8)]

        def banks_of(c0, c1):
            return range(c0 // 512, (c1 + 511) // 512)

        def t_pre(c0, c1):
            for b in banks_of(c0, c1):
                for (sk, v) in bank_r[b]:
                    wt("T", sk, v)

        def t_post(c0, c1):
            for b in banks_of(c0, c1):
                bank_w[b] = cnt["sT"]
                bank_r[b] = []

        def r_pre(ek, c0, c1):
            for b in banks_of(c0, c1):
                wt(ek, "sT", bank_w[b])

        def r_post(sk, c0, c1):
            for b in banks_of(c0, c1):
                bank_r[b].append((sk, cnt[sk]))

        def barrier(level=1):
            if paranoid < level:
                return
            for ek in ("T", "V", "S", "G", "Y"):
                for sk in cnt:
                    wt(ek, sk, cnt[sk])
                for sk, v in quse.items():
                    wt(ek, sk, v)

        _stats_cols = {}

        def scol(name):
            if name not in _stats_cols:
                _stats_cols[name] = len(_stats_cols)
                assert len(_stats_cols) <= 128, "stats arena overflow"
            i = _stats_cols[name]
            return stats[:, i:i + 1]

        C_EPS = consts[:, 0:1]
        C_NEG1 = consts[:, 1:2]
        C_NEGH = consts[:, 2:3]

        def ln_stats(xt, tag):
            s1, s2 = scol(tag + "_s1"), scol(tag + "_s2")
            mu, nv = scol(tag + "_mu"), scol(tag + "_nv")
            r, nm = scol(tag + "_r"), scol(tag + "_nm")
            inc(nc.vector.reduce_sum(s1, xt, axis=AX.X), "sV")
            inc(nc.vector.scalar_tensor_tensor(cbuf[:, :], xt, 1.0, xt,
                                               ALU.mult, ALU.mult,
                                               accum_out=s2), "sV")
            wt("V", "sV", cnt["sV"])
            inc(nc.vector.tensor_scalar(mu, s1, 1.0 / D, None, ALU.mult), "sV")
            inc(nc.vector.tensor_scalar(nv, s2, 1.0 / D, None, ALU.mult), "sV")
            inc(nc.vector.tensor_scalar(nv, mu, mu, nv, ALU.mult, ALU.subtract), "sV")
            wt("S", "sV", cnt["sV"])
            inc(nc.scalar.activation(r, nv, AF.Ln, bias=C_EPS, scale=C_NEG1), "sS")
            inc(nc.scalar.activation(r, r, AF.Exp, bias=0.0, scale=C_NEGH), "sS")
            wt("V", "sS", cnt["sS"])
            inc(nc.vector.tensor_scalar(nm, mu, r, -1.0, ALU.mult, ALU.mult), "sV")
            return r, nm

        def quant_cast(src, dst_bf16, inv_col, par):
            it = i8t[:, par * D:(par + 1) * D]
            inc(nc.scalar.activation(it, src, AF.Identity,
                                     bias=0.0, scale=inv_col), "sS")
            src_free = cnt["sS"]
            inc(nc.scalar.activation(dst_bf16, it, AF.Identity,
                                     bias=0.0, scale=1.0), "sS")
            return src_free

        inc(nc.vector.memset(consts[:, 0:1], EPS), "sV")
        inc(nc.vector.memset(consts[:, 1:2], -1.0), "sV")
        inc(nc.vector.memset(consts[:, 2:3], -0.5), "sV")
        inc(nc.vector.memset(onesrow[:, :], 1.0), "sV")
        inc(nc.vector.memset(stats[:, :], 0.0), "sV")
        inc(nc.gpsimd.iota(cbuf[:, 0:128], [[1, 128]], channel_multiplier=-1,
                           allow_small_or_imprecise_dtypes=True), "sG")
        wt("V", "sG", cnt["sG"])
        inc(nc.vector.tensor_scalar(ident[:, :], cbuf[:, 0:128], 0.0, 1.0,
                                    ALU.is_equal, ALU.mult), "sV")
        ident_done = cnt["sV"]

        def h_transpose(j, par, ready_s):
            base = 2048 + (j % 4) * 512
            wt("T", "sS", ready_s)
            wt("T", "sV", ident_done)
            t_pre(base, base + 512)
            pview = ps[:, base:base + 512].bitcast(BF16)
            for k in range(DS):
                inc(nc.tensor.transpose(
                    pview[:, k * 128:(k + 1) * 128],
                    hstage[:, par * D + k * 128: par * D + (k + 1) * 128],
                    ident[:, :]), "sT")
            t_post(base, base + 512)
            hst_free = cnt["sT"]
            wt("S", "sT", hst_free)
            r_pre("S", base, base + 512)
            dst = arena_h[:, :].rearrange("p (s t) -> p s t",
                                          s=DS)[:, :, j * 128:(j + 1) * 128]
            inc(nc.scalar.copy(dst, pview.rearrange("p (s t) -> p s t", s=DS)),
                "sS")
            r_post("sS", base, base + 512)
            return hst_free, cnt["sS"]

        def load_w_part(dstbuf, src2d, waits=()):
            return dma("gq", dstbuf[:, :].rearrange("p (s c) -> p s c", s=DS),
                       src2d.rearrange("(s p) c -> p s c", p=128), waits=waits)

        wq_ld = [None] * NL
        wk_ld = [None] * NL
        wv_ld = [None] * NL
        wo_ld = [None] * NL
        wq_ld[0] = load_w_part(wbufA, wqkv[0, :, 0:D])
        wk_ld[0] = load_w_part(wbufB, wqkv[0, :, D:2 * D])

        fw_ld = [None] * DS
        for s in range(2):
            fw_ld[s] = dma("sq", aold[:, (s % 2) * D:(s % 2 + 1) * D],
                           fw[s * 128:(s + 1) * 128, :])
        hst_rd = [None, None]
        wqs_toks = []
        wss_toks = []
        wq_iv_free = [0, 0]
        for s in range(DS):
            par = s % 2
            fwt = aold[:, par * D:(par + 1) * D]
            am = scol("wq_am_%d" % par)
            iv = scol("wq_iv_%d" % par)
            wsc = scol("wq_ws_%d" % s)
            wtt("V", fw_ld[s])
            wt("V", "sS", wq_iv_free[par])
            inc(nc.vector.reduce_max(am, fwt, axis=AX.X,
                                     apply_absolute_value=True), "sV")
            wt("V", "sV", cnt["sV"])
            inc(nc.vector.tensor_single_scalar(am, am, 1e-8, ALU.max), "sV")
            inc(nc.vector.reciprocal(iv, am), "sV")
            wt("V", "sV", cnt["sV"])
            inc(nc.vector.tensor_scalar(wsc, am, 1.0 / 127.0, None, ALU.mult), "sV")
            inc(nc.vector.tensor_scalar(iv, iv, 127.0, None, ALU.mult), "sV")
            wt("V", "sV", cnt["sV"])
            ws_done = cnt["sV"]
            wt("S", "sV", ws_done)
            wtt("S", fw_ld[s])
            wtt("S", hst_rd[par])
            fwt_free = quant_cast(fwt, hstage[:, par * D:(par + 1) * D], iv, par)
            wq_iv_free[par] = fwt_free
            wq_done = cnt["sS"]
            if s + 2 < DS:
                fw_ld[s + 2] = dma("sq", fwt, fw[(s + 2) * 128:(s + 3) * 128, :],
                                   waits=(("sV", ws_done), ("sS", fwt_free)))
            tok = dma("gq", wq_hbm[s * 128:(s + 1) * 128, :],
                      hstage[:, par * D:(par + 1) * D],
                      waits=(("sS", wq_done),))
            hst_rd[par] = tok
            wqs_toks.append(tok)
            wss_toks.append(dma("gq",
                                ws_hbm[s * 128:(s + 1) * 128].rearrange("(p o) -> p o", o=1),
                                wsc, waits=(("sV", ws_done),)))

        x0_ld = [None] * TT
        for j in range(2):
            x0_ld[j] = dma("sq", ktmp[:, (j % 2) * D:(j % 2 + 1) * D],
                           x_in[j * 128:(j + 1) * 128, :])
        g_aw = [None] * TT
        g_hw = [None] * TT
        anew_rd = [None, None]
        s_happly = [0, 0]
        for j in range(TT):
            par = j % 2
            x0t = ktmp[:, par * D:(par + 1) * D]
            wtt("V", x0_ld[j])
            r1, nm1 = ln_stats(x0t, "i1_%d" % par)
            at = anew[:, par * D:(par + 1) * D]
            wtt("V", anew_rd[par])
            wt("V", "sS", s_happly[par])
            inc(nc.vector.tensor_scalar(at, x0t, r1, nm1, ALU.mult, ALU.add), "sV")
            a_prod = cnt["sV"]
            if j + 2 < TT:
                x0_ld[j + 2] = dma("sq", x0t, x_in[(j + 2) * 128:(j + 3) * 128, :],
                                   waits=(("sV", a_prod),))
            g_aw[j] = dma("gq", a_hbm[j * 128:(j + 1) * 128, :], at,
                          waits=(("sV", a_prod),))
            anew_rd[par] = g_aw[j]
            r2, nm2 = ln_stats(at, "i2_%d" % par)
            hst = hstage[:, par * D:(par + 1) * D]
            wtt("S", hst_rd[par])
            wt("S", "sV", cnt["sV"])
            inc(nc.scalar.activation(hst, at, AF.Identity,
                                     bias=nm2, scale=r2), "sS")
            s_happly[par] = cnt["sS"]
            hst_free, h_evac_done = h_transpose(j, par, s_happly[par])
            hst_rd[par] = ("sT", hst_free)

        def dbg_dump():
            for sk in cnt:
                wt("G", sk, cnt[sk])
            for sk, v in quse.items():
                wt("G", sk, v)
            dma("gq", y_out, a_hbm)
            dma("gq", ydbg, h_hbm)
            dma("gq", qdbg, q_hbm)
            dma("gq", atdbg, arena_at[:, :])
            dma("gq", sdbg, stats[:, :])
            dma("gq", odbg, arena_k[:, :])
            toks = [dma("gq", vdbg, arena_q[:, :])]
            for sk, v in quse.items():
                wt("G", sk, v)

        if stop == "init":
            dbg_dump()
            return nc

        xq_toks = []
        xs_done = [0] * TT
        xq_iv_free = [0, 0]
        t_endE_prev = 0
        t_endF_prev = 0
        wv_folded_prev = 0
        for l in range(NL):
            gcol_toks = []
            for s in range(DS):
                gcol_toks.append(dma("sq", gcol[:, s:s + 1],
                                     gamma[l, s * 128:(s + 1) * 128].rearrange("(p o) -> p o", o=1),
                                     waits=(("sV", wv_folded_prev),)))
            bcol_toks = []
            for s in range(DS):
                bcol_toks.append(dma("gq", betacol[:, s:s + 1],
                                     beta[l, s * 128:(s + 1) * 128].rearrange("(p o) -> p o", o=1)))
            brow_tok = dma("gq", boutrow[0:1, :], bout[l][None, :])

            wtl("V", gcol_toks)
            wtt("V", wq_ld[l])
            for s in range(DS):
                inc(nc.vector.tensor_scalar(wbufA[:, s * D:(s + 1) * D],
                                            wbufA[:, s * D:(s + 1) * D],
                                            gcol[:, s:s + 1], 0.0, ALU.mult, ALU.add), "sV")
            wq_folded = cnt["sV"]

            qst_rd = [None, None]
            q_toks = []
            for c in range(DS):
                base = 0 if (c % 2 == 0) else 2048
                wt("T", "sV", wq_folded)
                wt("T", "sS", h_evac_done)
                t_pre(base, base + 2048)
                for d in range(DS):
                    for nch in range(4):
                        c0 = base + nch * 512
                        ins = nc.tensor.matmul(
                            ps[:, c0:c0 + 512],
                            wbufA[:, d * D + c * 128: d * D + (c + 1) * 128],
                            arena_h[:, d * T + nch * 512: d * T + nch * 512 + 512],
                            start=(d == 0), stop=(d == DS - 1))
                inc(ins, "sT")
                t_post(base, base + 2048)
                par = c % 2
                nmx = scol("b1_nm_%d" % par)
                sm = scol("b1_s_%d" % par)
                ivs = scol("b1_iv_%d" % par)
                r_pre("V", base, base + 2048)
                inc(nc.vector.reduce_max(nmx, ps[:, base:base + 2048], axis=AX.X,
                                         negate=True), "sV")
                wt("S", "sV", cnt["sV"])
                r_pre("S", base, base + 2048)
                inc(nc.scalar.activation(eq[:, par * T:(par + 1) * T],
                                         ps[:, base:base + 2048], AF.Exp,
                                         bias=nmx, scale=1.0), "sS")
                r_post("sS", base, base + 2048)
                if stop is not None and l == 0:
                    dma("gq", eqall[c * 128:(c + 1) * 128, :],
                        eq[:, par * T:(par + 1) * T],
                        waits=(("sS", cnt["sS"]),))
                wt("V", "sS", cnt["sS"])
                inc(nc.vector.reduce_sum(sm, eq[:, par * T:(par + 1) * T],
                                         axis=AX.X), "sV")
                wt("V", "sV", cnt["sV"])
                inc(nc.vector.reciprocal(ivs, sm), "sV")
                wt("V", "sV", cnt["sV"])
                if stop is not None and l == 0:
                    inc(nc.vector.tensor_copy(scol("dbg_s_%d" % c), sm), "sV")
                    inc(nc.vector.tensor_copy(scol("dbg_iv_%d" % c), ivs), "sV")
                wtt("S", qst_rd[par])
                wt("S", "sV", cnt["sV"])
                inc(nc.scalar.activation(qstage[:, par * T:(par + 1) * T],
                                         eq[:, par * T:(par + 1) * T],
                                         AF.Identity, bias=0.0, scale=ivs), "sS")
                tok = dma("gq", q_hbm[c * 128:(c + 1) * 128, :],
                          qstage[:, par * T:(par + 1) * T],
                          waits=(("sS", cnt["sS"]),))
                qst_rd[par] = tok
                q_toks.append(tok)
                barrier(2)
            t_endB1 = cnt["sT"]
            barrier()

            wtt("T", wk_ld[l])
            wtl("T", bcol_toks)
            t_pre(0, 1024)
            for s in range(DS):
                for nch in range(2):
                    c0 = nch * 512
                    ins = nc.tensor.matmul(
                        ps[0:1, c0:c0 + 512], betacol[:, s:s + 1],
                        wbufB[:, s * D + nch * 512: s * D + nch * 512 + 512],
                        start=(s == 0), stop=(s == DS - 1))
            inc(ins, "sT")
            t_post(0, 1024)
            r_pre("V", 0, 1024)
            inc(nc.vector.tensor_copy(bwkrow[0:1, :], ps[0:1, 0:1024]), "sV")
            r_post("sV", 0, 1024)
            wtl("V", gcol_toks)
            for s in range(DS):
                inc(nc.vector.tensor_scalar(wbufB[:, s * D:(s + 1) * D],
                                            wbufB[:, s * D:(s + 1) * D],
                                            gcol[:, s:s + 1], 0.0, ALU.mult, ALU.add), "sV")
            wk_folded = cnt["sV"]
            barrier()

            wv_ld[l] = load_w_part(wbufA, wqkv[l, :, 2 * D:3 * D],
                                   waits=(("sT", t_endB1),))

            wt("Y", "sT", t_endE_prev)
            qT_toks = []
            for j in range(TT):
                qT_toks.append(dma("sq", arena_q[:, j * D:(j + 1) * D],
                                   q_hbm[:, j * 128:(j + 1) * 128],
                                   waits=q_toks, transpose=True))

            for j in range(TT):
                base = (j % 4) * 1024
                wt("T", "sV", wk_folded)
                t_pre(base, base + 1024)
                for nch in range(2):
                    nc.tensor.matmul(ps[:, base + nch * 512:base + nch * 512 + 512],
                                     onesrow[0:1, :],
                                     bwkrow[0:1, nch * 512:(nch + 1) * 512],
                                     start=True, stop=False)
                for d in range(DS):
                    for nch in range(2):
                        c0 = base + nch * 512
                        ins = nc.tensor.matmul(
                            ps[:, c0:c0 + 512],
                            arena_h[:, d * T + j * 128: d * T + (j + 1) * 128],
                            wbufB[:, d * D + nch * 512: d * D + nch * 512 + 512],
                            start=False, stop=(d == DS - 1))
                inc(ins, "sT")
                t_post(base, base + 1024)
                par = j % 2
                nmx = scol("b2_nm_%d" % par)
                sm = scol("b2_s_%d" % par)
                ivs = scol("b2_iv_%d" % par)
                r_pre("V", base, base + 1024)
                inc(nc.vector.reduce_max(nmx, ps[:, base:base + 1024], axis=AX.X,
                                         negate=True), "sV")
                wt("S", "sV", cnt["sV"])
                r_pre("S", base, base + 1024)
                inc(nc.scalar.activation(ktmp[:, par * D:(par + 1) * D],
                                         ps[:, base:base + 1024], AF.Exp,
                                         bias=nmx, scale=1.0), "sS")
                r_post("sS", base, base + 1024)
                wt("V", "sS", cnt["sS"])
                inc(nc.vector.reduce_sum(sm, ktmp[:, par * D:(par + 1) * D],
                                         axis=AX.X), "sV")
                wt("V", "sV", cnt["sV"])
                inc(nc.vector.reciprocal(ivs, sm), "sV")
                wt("V", "sV", cnt["sV"])
                wt("S", "sT", t_endF_prev)
                wt("S", "sV", cnt["sV"])
                inc(nc.scalar.activation(arena_k[:, j * D:(j + 1) * D],
                                         ktmp[:, par * D:(par + 1) * D],
                                         AF.Identity, bias=0.0, scale=ivs), "sS")
                barrier(2)
            t_endB2 = cnt["sT"]
            b2_v_done = cnt["sV"]
            b2_s_done = cnt["sS"]
            barrier()

            wtt("T", wv_ld[l])
            for c in range(DS):
                t_pre(0, 512)
                for s in range(DS):
                    ins = nc.tensor.matmul(
                        ps[:, c:c + 1],
                        wbufA[:, s * D + c * 128: s * D + (c + 1) * 128],
                        betacol[:, s:s + 1],
                        start=(s == 0), stop=(s == DS - 1))
                inc(ins, "sT")
                t_post(0, 512)
            r_pre("V", 0, 512)
            inc(nc.vector.tensor_copy(bwv[:, :], ps[:, 0:DS]), "sV")
            r_post("sV", 0, 512)
            bwv_evac = cnt["sV"]
            t_pre(0, 512)
            inc(nc.tensor.matmul(ps[:, 0:512], onesrow[0:1, :],
                                 bwkrow[0:1, 0:512], start=True, stop=True), "sT")
            t_post(0, 512)
            for s in range(DS):
                inc(nc.vector.tensor_scalar(wbufA[:, s * D:(s + 1) * D],
                                            wbufA[:, s * D:(s + 1) * D],
                                            gcol[:, s:s + 1], 0.0, ALU.mult, ALU.add), "sV")
            wv_folded = cnt["sV"]
            barrier()

            wo_ld[l] = load_w_part(wbufB, wout[l], waits=(("sT", t_endB2),))

            wtl("T", qT_toks)
            wt("T", "sV", b2_v_done)
            for m in range(DS):
                base = (m % 4) * 1024
                t_pre(base, base + 1024)
                for j in range(TT):
                    for nch in range(2):
                        c0 = base + nch * 512
                        ins = nc.tensor.matmul(
                            ps[:, c0:c0 + 512],
                            arena_k[:, j * D + m * 128: j * D + (m + 1) * 128],
                            arena_q[:, j * D + nch * 512: j * D + nch * 512 + 512],
                            start=(j == 0), stop=(j == TT - 1))
                inc(ins, "sT")
                t_post(base, base + 1024)
                wt("S", "sT", t_endE_prev)
                r_pre("S", base, base + 1024)
                inc(nc.scalar.copy(arena_at[:, m * D:(m + 1) * D],
                                   ps[:, base:base + 1024]), "sS")
                r_post("sS", base, base + 1024)
                barrier(2)
            t_endD = cnt["sT"]
            d_v_done = cnt["sV"]
            d_s_done = cnt["sS"]
            barrier()

            for c in range(DS):
                base = 0 if (c % 2 == 0) else 2048
                wt("T", "sV", wv_folded)
                t_pre(base, base + 2048)
                for d in range(DS):
                    for nch in range(4):
                        c0 = base + nch * 512
                        ins = nc.tensor.matmul(
                            ps[:, c0:c0 + 512],
                            wbufA[:, d * D + c * 128: d * D + (c + 1) * 128],
                            arena_h[:, d * T + nch * 512: d * T + nch * 512 + 512],
                            start=(d == 0), stop=(d == DS - 1))
                inc(ins, "sT")
                t_post(base, base + 2048)
                wt("S", "sT", t_endD)
                wt("S", "sV", bwv_evac)
                r_pre("S", base, base + 2048)
                inc(nc.scalar.add(arena_q[:, c * T:(c + 1) * T],
                                  ps[:, base:base + 2048],
                                  bwv[:, c:c + 1]), "sS")
                r_post("sS", base, base + 2048)
                barrier(2)
            t_endB3 = cnt["sT"]
            b3_v_done = cnt["sV"]
            b3_s_done = cnt["sS"]
            barrier()

            if l + 1 < NL:
                wq_ld[l + 1] = load_w_part(wbufA, wqkv[l + 1, :, 0:D],
                                           waits=(("sT", t_endB3),))

            wt("T", "sV", max(d_v_done, b3_v_done))
            wt("T", "sS", max(d_s_done, b3_s_done))
            for c in range(DS):
                base = 0 if (c % 2 == 0) else 2048
                t_pre(base, base + 2048)
                for m in range(DS):
                    for nch in range(4):
                        c0 = base + nch * 512
                        ins = nc.tensor.matmul(
                            ps[:, c0:c0 + 512],
                            arena_at[:, m * D + c * 128: m * D + (c + 1) * 128],
                            arena_q[:, m * T + nch * 512: m * T + nch * 512 + 512],
                            start=(m == 0), stop=(m == DS - 1))
                inc(ins, "sT")
                t_post(base, base + 2048)
                for half, ek, sk in ((0, "V", "sV"), (1, "S", "sS")):
                    c0 = base + half * 1024
                    wt(ek, "sT", t_endD)
                    r_pre(ek, c0, c0 + 1024)
                    dst = arena_k[:, c * T + half * 1024: c * T + half * 1024 + 1024]
                    if ek == "V":
                        inc(nc.vector.tensor_copy(dst, ps[:, c0:c0 + 1024]), sk)
                    else:
                        inc(nc.scalar.copy(dst, ps[:, c0:c0 + 1024]), sk)
                    r_post(sk, c0, c0 + 1024)
                barrier(2)
            t_endE = cnt["sT"]
            e_v_done = cnt["sV"]
            e_s_done = cnt["sS"]
            barrier()

            last = (l == NL - 1)
            a_ld = [None] * TT
            for j in range(2):
                a_ld[j] = dma("sq", aold[:, (j % 2) * D:(j % 2 + 1) * D],
                              a_hbm[j * 128:(j + 1) * 128, :],
                              waits=(g_aw[j],))
            if last:
                x1_ld = [None] * TT
                for j in range(2):
                    x1_ld[j] = dma("sq", ktmp[:, (j % 2) * D:(j % 2 + 1) * D],
                                   x_in[j * 128:(j + 1) * 128, :],
                                   waits=(("sV", b2_v_done), ("sS", b2_s_done)))
            sv_aold_done = [0, 0]
            sv_x0_done = [0, 0]
            for j in range(TT):
                base = (j % 2) * 1024
                par = j % 2
                wtt("T", wo_ld[l])
                wtt("T", brow_tok)
                wt("T", "sV", e_v_done)
                wt("T", "sS", e_s_done)
                t_pre(base, base + 1024)
                for nch in range(2):
                    nc.tensor.matmul(ps[:, base + nch * 512:base + nch * 512 + 512],
                                     onesrow[0:1, :],
                                     boutrow[0:1, nch * 512:(nch + 1) * 512],
                                     start=True, stop=False)
                for cc in range(DS):
                    for nch in range(2):
                        c0 = base + nch * 512
                        ins = nc.tensor.matmul(
                            ps[:, c0:c0 + 512],
                            arena_k[:, cc * T + j * 128: cc * T + (j + 1) * 128],
                            wbufB[:, cc * D + nch * 512: cc * D + nch * 512 + 512],
                            start=False, stop=(cc == DS - 1))
                inc(ins, "sT")
                t_post(base, base + 1024)
                at = anew[:, par * D:(par + 1) * D]
                wtt("V", a_ld[j])
                wtt("V", anew_rd[par])
                wt("V", "sS", s_happly[par])
                r_pre("V", base, base + 1024)
                inc(nc.vector.tensor_tensor(at, ps[:, base:base + 1024],
                                            aold[:, par * D:(par + 1) * D],
                                            ALU.add), "sV")
                r_post("sV", base, base + 1024)
                sv_aold_done[par] = cnt["sV"]
                if j + 2 < TT and not last:
                    a_ld[j + 2] = dma("sq", aold[:, par * D:(par + 1) * D],
                                      a_hbm[(j + 2) * 128:(j + 3) * 128, :],
                                      waits=(g_aw[j + 2],
                                             ("sV", sv_aold_done[par])))
                if not last:
                    g_aw[j] = dma("gq", a_hbm[j * 128:(j + 1) * 128, :], at,
                                  waits=(("sV", cnt["sV"]),))
                    anew_rd[par] = g_aw[j]
                    r2, nm2 = ln_stats(at, "f_%d" % par)
                    hst = hstage[:, par * D:(par + 1) * D]
                    wtt("S", hst_rd[par])
                    wt("S", "sV", cnt["sV"])
                    inc(nc.scalar.activation(hst, at, AF.Identity,
                                             bias=nm2, scale=r2), "sS")
                    s_happly[par] = cnt["sS"]
                    hst_free, h_evac_done = h_transpose(j, par, s_happly[par])
                    hst_rd[par] = ("sT", hst_free)
                else:
                    r2, nm2 = ln_stats(at, "f_%d" % par)
                    inc(nc.vector.tensor_scalar(eq[:, 0:D], at, r2, nm2,
                                                ALU.mult, ALU.add), "sV")
                    wtt("V", x1_ld[j])
                    inc(nc.vector.tensor_tensor(at, eq[:, 0:D],
                                                ktmp[:, par * D:(par + 1) * D],
                                                ALU.add), "sV")
                    sv_x0_done[par] = cnt["sV"]
                    if j + 2 < TT:
                        x1_ld[j + 2] = dma("sq", ktmp[:, par * D:(par + 1) * D],
                                           x_in[(j + 2) * 128:(j + 3) * 128, :],
                                           waits=(("sV", sv_x0_done[par]),))
                    g_aw[j] = dma("gq", a_hbm[j * 128:(j + 1) * 128, :], at,
                                  waits=(("sV", cnt["sV"]),))
                    anew_rd[par] = g_aw[j]
                    r4, nm4 = ln_stats(at, "ffn_%d" % par)
                    fq = aold[:, par * D:(par + 1) * D]
                    inc(nc.vector.tensor_scalar(fq, at, r4, nm4,
                                                ALU.mult, ALU.add), "sV")
                    am = scol("xq_am_%d" % par)
                    iv = scol("xq_iv_%d" % par)
                    wt("V", "sS", xq_iv_free[par])
                    inc(nc.vector.reduce_max(am, fq, axis=AX.X,
                                             apply_absolute_value=True), "sV")
                    wt("V", "sV", cnt["sV"])
                    inc(nc.vector.tensor_single_scalar(am, am, 1e-8, ALU.max), "sV")
                    inc(nc.vector.reciprocal(iv, am), "sV")
                    wt("V", "sV", cnt["sV"])
                    inc(nc.vector.tensor_scalar(xs[:, j:j + 1], am, 1.0 / 127.0,
                                                None, ALU.mult), "sV")
                    xs_done[j] = cnt["sV"]
                    inc(nc.vector.tensor_scalar(iv, iv, 127.0, None, ALU.mult), "sV")
                    wt("V", "sV", cnt["sV"])
                    wt("S", "sV", cnt["sV"])
                    wtt("S", hst_rd[par])
                    fq_free = quant_cast(fq, hstage[:, par * D:(par + 1) * D],
                                         iv, par)
                    xq_iv_free[par] = fq_free
                    tok = dma("gq", xq_hbm[j * 128:(j + 1) * 128, :],
                              hstage[:, par * D:(par + 1) * D],
                              waits=(("sS", cnt["sS"]),))
                    hst_rd[par] = tok
                    xq_toks.append(tok)
                    if j + 2 < TT:
                        a_ld[j + 2] = dma("sq", aold[:, par * D:(par + 1) * D],
                                          a_hbm[(j + 2) * 128:(j + 3) * 128, :],
                                          waits=(g_aw[j + 2],
                                                 ("sS", fq_free)))
                barrier(2)
            t_endF = cnt["sT"]
            barrier()

            if l + 1 < NL:
                wk_ld[l + 1] = load_w_part(wbufB, wqkv[l + 1, :, D:2 * D],
                                           waits=(("sT", t_endF),))

            t_endE_prev = t_endE
            t_endF_prev = t_endF
            wv_folded_prev = wv_folded
            if stop == ("L%d" % l):
                dbg_dump()
                return nc

        f_v_done = cnt["sV"]
        v_fin_free = cnt["sV"]
        wt("Y", "sT", t_endB3)
        xqT_w = [[] for _ in range(4)]
        for w in range(4):
            for s in range(DS):
                xqT_w[w].append(dma("sq",
                                    arena_h[:, s * T + w * 512: s * T + w * 512 + 512],
                                    xq_hbm[w * 512:(w + 1) * 512, s * 128:(s + 1) * 128],
                                    waits=xq_toks[4 * w:4 * w + 4], transpose=True))
        wqT_toks = []
        for s in range(DS):
            wqT_toks.append(dma("sq", wbufA[:, s * D:(s + 1) * D],
                                wq_hbm[:, s * 128:(s + 1) * 128],
                                waits=wqs_toks, transpose=True))
        wsb = eq[:, 0:D]
        fbb = eq[:, D:2 * D]
        wsb_tok = dma("sq", wsb, ws_hbm[None, :].partition_broadcast(128),
                      waits=list(wss_toks) + [("sV", v_fin_free)])
        fbb_tok = dma("sq", fbb, fb[None, :].partition_broadcast(128),
                      waits=[("sV", v_fin_free)])

        out_rd = [None, None]
        y_toks = []
        x1b_ld = [None] * TT
        for j in range(2):
            x1b_ld[j] = dma("sq", ktmp[:, (j % 2) * D:(j % 2 + 1) * D],
                            a_hbm[j * 128:(j + 1) * 128, :],
                            waits=(g_aw[j], ("sV", f_v_done)))
        sv_x1b_done = [0, 0]
        sv_fin_done = [0, 0]
        for j in range(TT):
            base = (j % 4) * 1024
            par = j % 2
            wtl("T", xqT_w[j // 4])
            wtl("T", wqT_toks)
            t_pre(base, base + 1024)
            for d in range(DS):
                for nch in range(2):
                    c0 = base + nch * 512
                    ins = nc.tensor.matmul(
                        ps[:, c0:c0 + 512],
                        arena_h[:, d * T + j * 128: d * T + (j + 1) * 128],
                        wbufA[:, d * D + nch * 512: d * D + nch * 512 + 512],
                        start=(d == 0), stop=(d == DS - 1))
            inc(ins, "sT")
            t_post(base, base + 1024)
            ft = anew[:, par * D:(par + 1) * D]
            wt("S", "sV", max(v_fin_free, xs_done[j], sv_fin_done[par]))
            r_pre("S", base, base + 1024)
            inc(nc.scalar.activation(ft, ps[:, base:base + 1024], AF.Copy,
                                     bias=0.0, scale=xs[:, j:j + 1]), "sS")
            r_post("sS", base, base + 1024)
            wt("V", "sS", cnt["sS"])
            wtt("V", wsb_tok)
            wtt("V", fbb_tok)
            inc(nc.vector.tensor_tensor(ft, ft, wsb, ALU.mult), "sV")
            inc(nc.vector.tensor_tensor(ft, ft, fbb, ALU.add), "sV")
            r5, nm5 = ln_stats(ft, "fo_%d" % par)
            inc(nc.vector.tensor_scalar(cbuf[:, :], ft, r5, nm5,
                                        ALU.mult, ALU.add), "sV")
            sv_fin_done[par] = cnt["sV"]
            ot = aold[:, par * D:(par + 1) * D]
            wtt("V", x1b_ld[j])
            wtt("V", out_rd[par])
            inc(nc.vector.tensor_tensor(ot, cbuf[:, :],
                                        ktmp[:, par * D:(par + 1) * D],
                                        ALU.add), "sV")
            sv_x1b_done[par] = cnt["sV"]
            if j + 2 < TT:
                x1b_ld[j + 2] = dma("sq", ktmp[:, par * D:(par + 1) * D],
                                    a_hbm[(j + 2) * 128:(j + 3) * 128, :],
                                    waits=(("sV", sv_x1b_done[par]),))
            tok = dma("gq", y_out[j * 128:(j + 1) * 128, :], ot,
                      waits=(("sV", cnt["sV"]),))
            out_rd[par] = tok
            y_toks.append(tok)
            barrier(2)

        if stop == "ffn":
            for sk in cnt:
                wt("G", sk, cnt[sk])
            for sk, v in quse.items():
                wt("G", sk, v)
            dma("gq", qdbg.rearrange("a b -> (a b)").rearrange("(a b) -> a b", a=T),
                xq_hbm)
            dma("gq", ydbg[0:D, :], wq_hbm)
            dma("gq", sdbg[:, 0:8], ws_hbm.rearrange("(p s) -> p s", p=128))
            for sk, v in quse.items():
                wt("G", sk, v)

        wtl("G", y_toks)

    return nc


_NC_CACHE = None


def kernel(L0_attn_x, L0_attn_w_qkv, L0_attn_w_out, L0_attn_b_out,
           L0_attn_gamma, L0_attn_beta, L0_ffn_weight, L0_ffn_bias):
    global _NC_CACHE
    if _NC_CACHE is None:
        _NC_CACHE = build_nc()
    nc = _NC_CACHE
    shared = {
        "wqkv": np.ascontiguousarray(L0_attn_w_qkv, dtype=np.float32),
        "wout": np.ascontiguousarray(L0_attn_w_out, dtype=np.float32),
        "bout": np.ascontiguousarray(L0_attn_b_out, dtype=np.float32),
        "gamma": np.ascontiguousarray(L0_attn_gamma, dtype=np.float32),
        "beta": np.ascontiguousarray(L0_attn_beta, dtype=np.float32),
        "fw": np.ascontiguousarray(L0_ffn_weight, dtype=np.float32),
        "fb": np.ascontiguousarray(L0_ffn_bias, dtype=np.float32),
    }
    x = np.ascontiguousarray(L0_attn_x, dtype=np.float32)
    in_maps = [dict(shared, x=x[b]) for b in range(8)]
    res = run_bass_kernel_spmd(nc, in_maps, core_ids=list(range(8)))
    return np.stack([np.asarray(r["y"], dtype=np.float32)
                     for r in res.results], axis=0)


# version 10
# speedup vs baseline: 1.1840x; 1.1840x over previous
import numpy as np
import ml_dtypes
import concourse.bass as bass
import concourse.mybir as mybir
from concourse.bass_utils import run_bass_kernel_spmd
from contextlib import ExitStack

F32 = mybir.dt.float32
BF16 = mybir.dt.bfloat16
I8 = mybir.dt.int8
AF = mybir.ActivationFunctionType
ALU = mybir.AluOpType
AX = mybir.AxisListType

T = 2048
D = 1024
NL = 6
TT = T // 128
DS = D // 128
EPS = 1e-5
NQS = 8


def build_nc(stop=None, paranoid=0):
    nc = bass.Bass("TRN2", target_bir_lowering=False)
    nc.detect_race_conditions = False

    x_in = nc.dram_tensor("x", [T, D], F32, kind="ExternalInput").ap()
    wqkv = nc.dram_tensor("wqkv", [NL, D, 3 * D], F32, kind="ExternalInput").ap()
    wout = nc.dram_tensor("wout", [NL, D, D], F32, kind="ExternalInput").ap()
    bout = nc.dram_tensor("bout", [NL, D], F32, kind="ExternalInput").ap()
    gamma = nc.dram_tensor("gamma", [NL, D], F32, kind="ExternalInput").ap()
    beta = nc.dram_tensor("beta", [NL, D], F32, kind="ExternalInput").ap()
    fw = nc.dram_tensor("fw", [D, D], F32, kind="ExternalInput").ap()
    fb = nc.dram_tensor("fb", [D], F32, kind="ExternalInput").ap()
    y_out = nc.dram_tensor("y", [T, D], F32, kind="ExternalOutput").ap()

    a_hbm = nc.dram_tensor("a_scr", [T, D], F32, kind="Internal").ap()
    h_hbm = nc.dram_tensor("h_scr", [T, D], BF16, kind="Internal").ap()
    q_hbm = nc.dram_tensor("q_scr", [D, T], BF16, kind="Internal").ap()
    xq_hbm = nc.dram_tensor("xq_scr", [T, D], BF16, kind="Internal").ap()
    wq_hbm = nc.dram_tensor("wq_scr", [D, D], BF16, kind="Internal").ap()
    ws_hbm = nc.dram_tensor("ws_scr", [D], F32, kind="Internal").ap()
    ydbg = qdbg = atdbg = odbg = vdbg = sdbg = None
    if stop is not None:
        ydbg = nc.dram_tensor("ydbg", [T, D], BF16, kind="ExternalOutput").ap()
        qdbg = nc.dram_tensor("qdbg", [D, T], BF16, kind="ExternalOutput").ap()
        atdbg = nc.dram_tensor("atdbg", [128, DS * D], BF16, kind="ExternalOutput").ap()
        odbg = nc.dram_tensor("odbg", [128, TT * D], BF16, kind="ExternalOutput").ap()
        vdbg = nc.dram_tensor("vdbg", [128, TT * D], BF16, kind="ExternalOutput").ap()
        sdbg = nc.dram_tensor("sdbg", [128, 128], F32, kind="ExternalOutput").ap()
        eqall = nc.dram_tensor("eqall", [D, T], F32, kind="ExternalOutput").ap()

    st = ExitStack()
    with st:
        sb = lambda name, shape, dt: st.enter_context(nc.sbuf_tensor(name, shape, dt))
        arena_h = sb("arena_h", [128, DS * T], BF16)
        arena_q = sb("arena_q", [128, TT * D], BF16)
        arena_k = sb("arena_k", [128, TT * D], BF16)
        arena_at = sb("arena_at", [128, DS * D], BF16)
        wbufA = sb("wbufA", [128, DS * D], BF16)
        wbufB = sb("wbufB", [128, DS * D], BF16)
        eq = sb("eq", [128, 2 * T], F32)
        qstage = sb("qstage", [128, 2 * T], BF16)
        ktmp = sb("ktmp", [128, 2 * D], F32)
        aold = sb("aold", [128, 2 * D], F32)
        anew = sb("anew", [128, 2 * D], F32)
        hstage = sb("hstage", [128, 2 * D], BF16)
        i8t = sb("i8t", [128, 2 * D], I8)
        ident = sb("ident", [128, 128], BF16)
        cbuf = sb("cbuf", [128, D], F32)
        bwkrow = sb("bwkrow", [1, D], BF16)
        boutrow = sb("boutrow", [1, D], BF16)
        onesrow = sb("onesrow", [1, 128], BF16)
        bwv = sb("bwv", [128, DS], F32)
        betacol = sb("betacol", [128, DS], BF16)
        gcol = sb("gcol", [128, DS], F32)
        xs = sb("xs", [128, TT], F32)
        stats = sb("stats", [128, 128], F32)
        consts = sb("consts", [128, 4], F32)
        ps = st.enter_context(nc.psum_tensor("ps", [128, 4096], F32))

        sems = {}
        sem_names = (["sT", "sV", "sS", "sG"]
                     + ["gq%d" % i for i in range(NQS)]
                     + ["sq%d" % i for i in range(NQS)])
        for name in sem_names:
            sems[name] = st.enter_context(nc.semaphore(name))
        cnt = {"sT": 0, "sV": 0, "sS": 0, "sG": 0}
        quse = {name: 0 for name in sem_names if name not in cnt}
        qrr = {"gq": 0, "sq": 0}
        seen = {}
        ENG = {"T": nc.tensor, "V": nc.vector, "S": nc.scalar,
               "G": nc.gpsimd, "Y": nc.sync}

        def wt(ek, sk, val):
            if sk is None or val is None or val <= 0:
                return
            key = (ek, sk)
            if seen.get(key, 0) >= val:
                return
            ENG[ek].wait_ge(sems[sk], val)
            seen[key] = val

        def wtt(ek, tok):
            if tok:
                wt(ek, tok[0], tok[1])

        def wtl(ek, toks):
            for tok in toks:
                wtt(ek, tok)

        def inc(ins, sk, n=1):
            ins.then_inc(sems[sk], n)
            cnt[sk] += n
            return cnt[sk]

        def dma(qk, out, in_, waits=(), transpose=False):
            ek = "G" if qk == "gq" else "Y"
            i = qrr[qk]
            qrr[qk] = (i + 1) % NQS
            sname = qk + str(i)
            prev = quse[sname]
            wt(ek, sname, prev)
            for tok in waits:
                wtt(ek, tok)
            if transpose:
                ins = ENG[ek].dma_start_transpose(out, in_)
            else:
                ins = ENG[ek].dma_start(out, in_)
            ins.then_inc(sems[sname], 16)
            quse[sname] = prev + 16
            return (sname, prev + 16)

        bank_w = [0] * 8
        bank_r = [[] for _ in range(8)]

        def banks_of(c0, c1):
            return range(c0 // 512, (c1 + 511) // 512)

        def t_pre(c0, c1):
            for b in banks_of(c0, c1):
                for (sk, v) in bank_r[b]:
                    wt("T", sk, v)

        def t_post(c0, c1):
            for b in banks_of(c0, c1):
                bank_w[b] = cnt["sT"]
                bank_r[b] = []

        def r_pre(ek, c0, c1):
            for b in banks_of(c0, c1):
                wt(ek, "sT", bank_w[b])

        def r_post(sk, c0, c1):
            for b in banks_of(c0, c1):
                bank_r[b].append((sk, cnt[sk]))

        def barrier(level=1):
            if paranoid < level:
                return
            for ek in ("T", "V", "S", "G", "Y"):
                for sk in cnt:
                    wt(ek, sk, cnt[sk])
                for sk, v in quse.items():
                    wt(ek, sk, v)

        _stats_cols = {}

        def scol(name):
            if name not in _stats_cols:
                _stats_cols[name] = len(_stats_cols)
                assert len(_stats_cols) <= 128, "stats arena overflow"
            i = _stats_cols[name]
            return stats[:, i:i + 1]

        C_EPS = consts[:, 0:1]
        C_NEG1 = consts[:, 1:2]
        C_NEGH = consts[:, 2:3]

        def ln_stats(xt, tag):
            s1, s2 = scol(tag + "_s1"), scol(tag + "_s2")
            mu, nv = scol(tag + "_mu"), scol(tag + "_nv")
            r, nm = scol(tag + "_r"), scol(tag + "_nm")
            inc(nc.vector.reduce_sum(s1, xt, axis=AX.X), "sV")
            inc(nc.vector.scalar_tensor_tensor(cbuf[:, :], xt, 1.0, xt,
                                               ALU.mult, ALU.mult,
                                               accum_out=s2), "sV")
            wt("V", "sV", cnt["sV"])
            inc(nc.vector.tensor_scalar(mu, s1, 1.0 / D, None, ALU.mult), "sV")
            inc(nc.vector.tensor_scalar(nv, s2, 1.0 / D, None, ALU.mult), "sV")
            inc(nc.vector.tensor_scalar(nv, mu, mu, nv, ALU.mult, ALU.subtract), "sV")
            wt("S", "sV", cnt["sV"])
            inc(nc.scalar.activation(r, nv, AF.Ln, bias=C_EPS, scale=C_NEG1), "sS")
            inc(nc.scalar.activation(r, r, AF.Exp, bias=0.0, scale=C_NEGH), "sS")
            wt("V", "sS", cnt["sS"])
            inc(nc.vector.tensor_scalar(nm, mu, r, -1.0, ALU.mult, ALU.mult), "sV")
            return r, nm

        def quant_cast(src, dst_bf16, inv_col, par):
            it = i8t[:, par * D:(par + 1) * D]
            inc(nc.scalar.activation(it, src, AF.Identity,
                                     bias=0.0, scale=inv_col), "sS")
            src_free = cnt["sS"]
            inc(nc.scalar.activation(dst_bf16, it, AF.Identity,
                                     bias=0.0, scale=1.0), "sS")
            return src_free

        inc(nc.vector.memset(consts[:, 0:1], EPS), "sV")
        inc(nc.vector.memset(consts[:, 1:2], -1.0), "sV")
        inc(nc.vector.memset(consts[:, 2:3], -0.5), "sV")
        inc(nc.vector.memset(onesrow[:, :], 1.0), "sV")
        inc(nc.vector.memset(stats[:, :], 0.0), "sV")
        inc(nc.gpsimd.iota(cbuf[:, 0:128], [[1, 128]], channel_multiplier=-1,
                           allow_small_or_imprecise_dtypes=True), "sG")
        wt("V", "sG", cnt["sG"])
        inc(nc.vector.tensor_scalar(ident[:, :], cbuf[:, 0:128], 0.0, 1.0,
                                    ALU.is_equal, ALU.mult), "sV")
        ident_done = cnt["sV"]

        def h_transpose(j, par, ready_s):
            base = 2048 + (j % 4) * 512
            wt("T", "sS", ready_s)
            wt("T", "sV", ident_done)
            t_pre(base, base + 512)
            pview = ps[:, base:base + 512].bitcast(BF16)
            for k in range(DS):
                inc(nc.tensor.transpose(
                    pview[:, k * 128:(k + 1) * 128],
                    hstage[:, par * D + k * 128: par * D + (k + 1) * 128],
                    ident[:, :]), "sT")
            t_post(base, base + 512)
            hst_free = cnt["sT"]
            wt("S", "sT", hst_free)
            r_pre("S", base, base + 512)
            dst = arena_h[:, :].rearrange("p (s t) -> p s t",
                                          s=DS)[:, :, j * 128:(j + 1) * 128]
            inc(nc.scalar.copy(dst, pview.rearrange("p (s t) -> p s t", s=DS)),
                "sS")
            r_post("sS", base, base + 512)
            return hst_free, cnt["sS"]

        def load_w_part(dstbuf, src2d, waits=()):
            return dma("gq", dstbuf[:, :].rearrange("p (s c) -> p s c", s=DS),
                       src2d.rearrange("(s p) c -> p s c", p=128), waits=waits)

        wq_ld = [None] * NL
        wk_ld = [None] * NL
        wv_ld = [None] * NL
        wo_ld = [None] * NL
        wq_ld[0] = load_w_part(wbufA, wqkv[0, :, 0:D])
        wk_ld[0] = load_w_part(wbufB, wqkv[0, :, D:2 * D])

        fw_ld = [None] * DS
        for s in range(2):
            fw_ld[s] = dma("sq", aold[:, (s % 2) * D:(s % 2 + 1) * D],
                           fw[s * 128:(s + 1) * 128, :])
        hst_rd = [None, None]
        wqs_toks = []
        wss_toks = []
        wq_iv_free = [0, 0]
        for s in range(DS):
            par = s % 2
            fwt = aold[:, par * D:(par + 1) * D]
            am = scol("wq_am_%d" % par)
            iv = scol("wq_iv_%d" % par)
            wsc = scol("wq_ws_%d" % s)
            wtt("V", fw_ld[s])
            wt("V", "sS", wq_iv_free[par])
            inc(nc.vector.reduce_max(am, fwt, axis=AX.X,
                                     apply_absolute_value=True), "sV")
            wt("V", "sV", cnt["sV"])
            inc(nc.vector.tensor_single_scalar(am, am, 1e-8, ALU.max), "sV")
            inc(nc.vector.reciprocal(iv, am), "sV")
            wt("V", "sV", cnt["sV"])
            inc(nc.vector.tensor_scalar(wsc, am, 1.0 / 127.0, None, ALU.mult), "sV")
            inc(nc.vector.tensor_scalar(iv, iv, 127.0, None, ALU.mult), "sV")
            wt("V", "sV", cnt["sV"])
            ws_done = cnt["sV"]
            wt("S", "sV", ws_done)
            wtt("S", fw_ld[s])
            wtt("S", hst_rd[par])
            fwt_free = quant_cast(fwt, hstage[:, par * D:(par + 1) * D], iv, par)
            wq_iv_free[par] = fwt_free
            wq_done = cnt["sS"]
            if s + 2 < DS:
                fw_ld[s + 2] = dma("sq", fwt, fw[(s + 2) * 128:(s + 3) * 128, :],
                                   waits=(("sV", ws_done), ("sS", fwt_free)))
            tok = dma("gq", wq_hbm[s * 128:(s + 1) * 128, :],
                      hstage[:, par * D:(par + 1) * D],
                      waits=(("sS", wq_done),))
            hst_rd[par] = tok
            wqs_toks.append(tok)
            wss_toks.append(dma("gq",
                                ws_hbm[s * 128:(s + 1) * 128].rearrange("(p o) -> p o", o=1),
                                wsc, waits=(("sV", ws_done),)))

        x0_ld = [None] * TT
        for j in range(2):
            x0_ld[j] = dma("sq", ktmp[:, (j % 2) * D:(j % 2 + 1) * D],
                           x_in[j * 128:(j + 1) * 128, :])
        g_aw = [None] * TT
        g_hw = [None] * TT
        anew_rd = [None, None]
        s_happly = [0, 0]
        for j in range(TT):
            par = j % 2
            x0t = ktmp[:, par * D:(par + 1) * D]
            wtt("V", x0_ld[j])
            r1, nm1 = ln_stats(x0t, "i1_%d" % par)
            at = anew[:, par * D:(par + 1) * D]
            wtt("V", anew_rd[par])
            wt("V", "sS", s_happly[par])
            inc(nc.vector.tensor_scalar(at, x0t, r1, nm1, ALU.mult, ALU.add), "sV")
            a_prod = cnt["sV"]
            if j + 2 < TT:
                x0_ld[j + 2] = dma("sq", x0t, x_in[(j + 2) * 128:(j + 3) * 128, :],
                                   waits=(("sV", a_prod),))
            g_aw[j] = dma("gq", a_hbm[j * 128:(j + 1) * 128, :], at,
                          waits=(("sV", a_prod),))
            anew_rd[par] = g_aw[j]
            r2, nm2 = ln_stats(at, "i2_%d" % par)
            hst = hstage[:, par * D:(par + 1) * D]
            wtt("S", hst_rd[par])
            wt("S", "sV", cnt["sV"])
            inc(nc.scalar.activation(hst, at, AF.Identity,
                                     bias=nm2, scale=r2), "sS")
            s_happly[par] = cnt["sS"]
            hst_free, h_evac_done = h_transpose(j, par, s_happly[par])
            hst_rd[par] = ("sT", hst_free)

        def dbg_dump():
            for sk in cnt:
                wt("G", sk, cnt[sk])
            for sk, v in quse.items():
                wt("G", sk, v)
            dma("gq", y_out, a_hbm)
            dma("gq", ydbg, h_hbm)
            dma("gq", qdbg, q_hbm)
            dma("gq", atdbg, arena_at[:, :])
            dma("gq", sdbg, stats[:, :])
            dma("gq", odbg, arena_k[:, :])
            toks = [dma("gq", vdbg, arena_q[:, :])]
            for sk, v in quse.items():
                wt("G", sk, v)

        if stop == "init":
            dbg_dump()
            return nc

        xq_toks = []
        xs_done = [0] * TT
        xq_iv_free = [0, 0]
        t_endE_prev = 0
        t_endF_prev = 0
        wv_folded_prev = 0
        for l in range(NL):
            gcol_toks = []
            for s in range(DS):
                gcol_toks.append(dma("sq", gcol[:, s:s + 1],
                                     gamma[l, s * 128:(s + 1) * 128].rearrange("(p o) -> p o", o=1),
                                     waits=(("sV", wv_folded_prev),)))
            bcol_toks = []
            for s in range(DS):
                bcol_toks.append(dma("gq", betacol[:, s:s + 1],
                                     beta[l, s * 128:(s + 1) * 128].rearrange("(p o) -> p o", o=1)))
            brow_tok = dma("gq", boutrow[0:1, :], bout[l][None, :])

            wtl("V", gcol_toks)
            wtt("V", wq_ld[l])
            for s in range(DS):
                inc(nc.vector.tensor_scalar(wbufA[:, s * D:(s + 1) * D],
                                            wbufA[:, s * D:(s + 1) * D],
                                            gcol[:, s:s + 1], 0.0, ALU.mult, ALU.add), "sV")
            wq_folded = cnt["sV"]

            qst_rd = [None, None]
            q_toks = []
            for c in range(DS):
                base = 0 if (c % 2 == 0) else 2048
                wt("T", "sV", wq_folded)
                wt("T", "sS", h_evac_done)
                t_pre(base, base + 2048)
                for d in range(DS):
                    for nch in range(4):
                        c0 = base + nch * 512
                        ins = nc.tensor.matmul(
                            ps[:, c0:c0 + 512],
                            wbufA[:, d * D + c * 128: d * D + (c + 1) * 128],
                            arena_h[:, d * T + nch * 512: d * T + nch * 512 + 512],
                            start=(d == 0), stop=(d == DS - 1))
                inc(ins, "sT")
                t_post(base, base + 2048)
                par = c % 2
                nmx = scol("b1_nm_%d" % par)
                sm = scol("b1_s_%d" % par)
                ivs = scol("b1_iv_%d" % par)
                r_pre("V", base, base + 2048)
                inc(nc.vector.reduce_max(nmx, ps[:, base:base + 2048], axis=AX.X,
                                         negate=True), "sV")
                wt("S", "sV", cnt["sV"])
                r_pre("S", base, base + 2048)
                inc(nc.scalar.activation(eq[:, par * T:(par + 1) * T],
                                         ps[:, base:base + 2048], AF.Exp,
                                         bias=nmx, scale=1.0), "sS")
                r_post("sS", base, base + 2048)
                if stop is not None and l == 0:
                    dma("gq", eqall[c * 128:(c + 1) * 128, :],
                        eq[:, par * T:(par + 1) * T],
                        waits=(("sS", cnt["sS"]),))
                wt("V", "sS", cnt["sS"])
                inc(nc.vector.reduce_sum(sm, eq[:, par * T:(par + 1) * T],
                                         axis=AX.X), "sV")
                wt("V", "sV", cnt["sV"])
                inc(nc.vector.reciprocal(ivs, sm), "sV")
                wt("V", "sV", cnt["sV"])
                if stop is not None and l == 0:
                    inc(nc.vector.tensor_copy(scol("dbg_s_%d" % c), sm), "sV")
                    inc(nc.vector.tensor_copy(scol("dbg_iv_%d" % c), ivs), "sV")
                wtt("S", qst_rd[par])
                wt("S", "sV", cnt["sV"])
                inc(nc.scalar.activation(qstage[:, par * T:(par + 1) * T],
                                         eq[:, par * T:(par + 1) * T],
                                         AF.Identity, bias=0.0, scale=ivs), "sS")
                tok = dma("gq", q_hbm[c * 128:(c + 1) * 128, :],
                          qstage[:, par * T:(par + 1) * T],
                          waits=(("sS", cnt["sS"]),))
                qst_rd[par] = tok
                q_toks.append(tok)
                barrier(2)
            t_endB1 = cnt["sT"]
            barrier()

            wtt("T", wk_ld[l])
            wtl("T", bcol_toks)
            t_pre(0, 1024)
            for s in range(DS):
                for nch in range(2):
                    c0 = nch * 512
                    ins = nc.tensor.matmul(
                        ps[0:1, c0:c0 + 512], betacol[:, s:s + 1],
                        wbufB[:, s * D + nch * 512: s * D + nch * 512 + 512],
                        start=(s == 0), stop=(s == DS - 1))
            inc(ins, "sT")
            t_post(0, 1024)
            r_pre("V", 0, 1024)
            inc(nc.vector.tensor_copy(bwkrow[0:1, :], ps[0:1, 0:1024]), "sV")
            r_post("sV", 0, 1024)
            wtl("V", gcol_toks)
            for s in range(DS):
                inc(nc.vector.tensor_scalar(wbufB[:, s * D:(s + 1) * D],
                                            wbufB[:, s * D:(s + 1) * D],
                                            gcol[:, s:s + 1], 0.0, ALU.mult, ALU.add), "sV")
            wk_folded = cnt["sV"]
            barrier()

            wv_ld[l] = load_w_part(wbufA, wqkv[l, :, 2 * D:3 * D],
                                   waits=(("sT", t_endB1),))

            wt("Y", "sT", t_endE_prev)
            qT_toks = []
            for j in range(TT):
                qT_toks.append(dma("sq", arena_q[:, j * D:(j + 1) * D],
                                   q_hbm[:, j * 128:(j + 1) * 128],
                                   waits=q_toks, transpose=True))

            for j in range(TT):
                base = (j % 4) * 1024
                wt("T", "sV", wk_folded)
                t_pre(base, base + 1024)
                for nch in range(2):
                    nc.tensor.matmul(ps[:, base + nch * 512:base + nch * 512 + 512],
                                     onesrow[0:1, :],
                                     bwkrow[0:1, nch * 512:(nch + 1) * 512],
                                     start=True, stop=False)
                for d in range(DS):
                    for nch in range(2):
                        c0 = base + nch * 512
                        ins = nc.tensor.matmul(
                            ps[:, c0:c0 + 512],
                            arena_h[:, d * T + j * 128: d * T + (j + 1) * 128],
                            wbufB[:, d * D + nch * 512: d * D + nch * 512 + 512],
                            start=False, stop=(d == DS - 1))
                inc(ins, "sT")
                t_post(base, base + 1024)
                par = j % 2
                nmx = scol("b2_nm_%d" % par)
                sm = scol("b2_s_%d" % par)
                ivs = scol("b2_iv_%d" % par)
                r_pre("V", base, base + 1024)
                inc(nc.vector.reduce_max(nmx, ps[:, base:base + 1024], axis=AX.X,
                                         negate=True), "sV")
                wt("S", "sV", cnt["sV"])
                r_pre("S", base, base + 1024)
                inc(nc.scalar.activation(ktmp[:, par * D:(par + 1) * D],
                                         ps[:, base:base + 1024], AF.Exp,
                                         bias=nmx, scale=1.0), "sS")
                r_post("sS", base, base + 1024)
                wt("V", "sS", cnt["sS"])
                inc(nc.vector.reduce_sum(sm, ktmp[:, par * D:(par + 1) * D],
                                         axis=AX.X), "sV")
                wt("V", "sV", cnt["sV"])
                inc(nc.vector.reciprocal(ivs, sm), "sV")
                wt("V", "sV", cnt["sV"])
                wt("S", "sT", t_endF_prev)
                wt("S", "sV", cnt["sV"])
                inc(nc.scalar.activation(arena_k[:, j * D:(j + 1) * D],
                                         ktmp[:, par * D:(par + 1) * D],
                                         AF.Identity, bias=0.0, scale=ivs), "sS")
                barrier(2)
            t_endB2 = cnt["sT"]
            b2_v_done = cnt["sV"]
            b2_s_done = cnt["sS"]
            barrier()

            wtt("T", wv_ld[l])
            for c in range(DS):
                t_pre(0, 512)
                for s in range(DS):
                    ins = nc.tensor.matmul(
                        ps[:, c:c + 1],
                        wbufA[:, s * D + c * 128: s * D + (c + 1) * 128],
                        betacol[:, s:s + 1],
                        start=(s == 0), stop=(s == DS - 1))
                inc(ins, "sT")
                t_post(0, 512)
            r_pre("V", 0, 512)
            inc(nc.vector.tensor_copy(bwv[:, :], ps[:, 0:DS]), "sV")
            r_post("sV", 0, 512)
            bwv_evac = cnt["sV"]
            t_pre(0, 512)
            inc(nc.tensor.matmul(ps[:, 0:512], onesrow[0:1, :],
                                 bwkrow[0:1, 0:512], start=True, stop=True), "sT")
            t_post(0, 512)
            for s in range(DS):
                inc(nc.vector.tensor_scalar(wbufA[:, s * D:(s + 1) * D],
                                            wbufA[:, s * D:(s + 1) * D],
                                            gcol[:, s:s + 1], 0.0, ALU.mult, ALU.add), "sV")
            wv_folded = cnt["sV"]
            barrier()

            wo_ld[l] = load_w_part(wbufB, wout[l], waits=(("sT", t_endB2),))

            wtl("T", qT_toks)
            wt("T", "sV", b2_v_done)
            for m in range(DS):
                base = (m % 4) * 1024
                t_pre(base, base + 1024)
                for j in range(TT):
                    for nch in range(2):
                        c0 = base + nch * 512
                        ins = nc.tensor.matmul(
                            ps[:, c0:c0 + 512],
                            arena_k[:, j * D + m * 128: j * D + (m + 1) * 128],
                            arena_q[:, j * D + nch * 512: j * D + nch * 512 + 512],
                            start=(j == 0), stop=(j == TT - 1))
                inc(ins, "sT")
                t_post(base, base + 1024)
                wt("S", "sT", t_endE_prev)
                r_pre("S", base, base + 1024)
                inc(nc.scalar.copy(arena_at[:, m * D:(m + 1) * D],
                                   ps[:, base:base + 1024]), "sS")
                r_post("sS", base, base + 1024)
                barrier(2)
            t_endD = cnt["sT"]
            d_v_done = cnt["sV"]
            d_s_done = cnt["sS"]
            barrier()

            for c in range(DS):
                base = 0 if (c % 2 == 0) else 2048
                wt("T", "sV", wv_folded)
                t_pre(base, base + 2048)
                for d in range(DS):
                    for nch in range(4):
                        c0 = base + nch * 512
                        ins = nc.tensor.matmul(
                            ps[:, c0:c0 + 512],
                            wbufA[:, d * D + c * 128: d * D + (c + 1) * 128],
                            arena_h[:, d * T + nch * 512: d * T + nch * 512 + 512],
                            start=(d == 0), stop=(d == DS - 1))
                inc(ins, "sT")
                t_post(base, base + 2048)
                wt("S", "sT", t_endD)
                wt("S", "sV", bwv_evac)
                r_pre("S", base, base + 2048)
                inc(nc.scalar.add(arena_q[:, c * T:(c + 1) * T],
                                  ps[:, base:base + 2048],
                                  bwv[:, c:c + 1]), "sS")
                r_post("sS", base, base + 2048)
                barrier(2)
            t_endB3 = cnt["sT"]
            b3_v_done = cnt["sV"]
            b3_s_done = cnt["sS"]
            barrier()

            if l + 1 < NL:
                wq_ld[l + 1] = load_w_part(wbufA, wqkv[l + 1, :, 0:D],
                                           waits=(("sT", t_endB3),))

            wt("T", "sV", max(d_v_done, b3_v_done))
            wt("T", "sS", max(d_s_done, b3_s_done))
            for c in range(DS):
                base = 0 if (c % 2 == 0) else 2048
                t_pre(base, base + 2048)
                for m in range(DS):
                    for nch in range(4):
                        c0 = base + nch * 512
                        ins = nc.tensor.matmul(
                            ps[:, c0:c0 + 512],
                            arena_at[:, m * D + c * 128: m * D + (c + 1) * 128],
                            arena_q[:, m * T + nch * 512: m * T + nch * 512 + 512],
                            start=(m == 0), stop=(m == DS - 1))
                inc(ins, "sT")
                t_post(base, base + 2048)
                for half, ek, sk in ((0, "V", "sV"), (1, "S", "sS")):
                    c0 = base + half * 1024
                    wt(ek, "sT", t_endD)
                    r_pre(ek, c0, c0 + 1024)
                    dst = arena_k[:, c * T + half * 1024: c * T + half * 1024 + 1024]
                    if ek == "V":
                        inc(nc.vector.tensor_copy(dst, ps[:, c0:c0 + 1024]), sk)
                    else:
                        inc(nc.scalar.copy(dst, ps[:, c0:c0 + 1024]), sk)
                    r_post(sk, c0, c0 + 1024)
                barrier(2)
            t_endE = cnt["sT"]
            e_v_done = cnt["sV"]
            e_s_done = cnt["sS"]
            barrier()

            last = (l == NL - 1)
            a_ld = [None] * TT
            for j in range(2):
                a_ld[j] = dma("sq", aold[:, (j % 2) * D:(j % 2 + 1) * D],
                              a_hbm[j * 128:(j + 1) * 128, :],
                              waits=(g_aw[j],))
            if last:
                x1_ld = [None] * TT
                for j in range(2):
                    x1_ld[j] = dma("sq", ktmp[:, (j % 2) * D:(j % 2 + 1) * D],
                                   x_in[j * 128:(j + 1) * 128, :],
                                   waits=(("sV", b2_v_done), ("sS", b2_s_done)))
            sv_aold_done = [0, 0]
            sv_x0_done = [0, 0]
            for j in range(TT):
                base = (j % 2) * 1024
                par = j % 2
                if not last and j >= 2:
                    hst_free, h_evac_done = h_transpose(
                        j - 2, (j - 2) % 2, s_happly[(j - 2) % 2])
                    hst_rd[(j - 2) % 2] = ("sT", hst_free)
                wtt("T", wo_ld[l])
                wtt("T", brow_tok)
                wt("T", "sV", e_v_done)
                wt("T", "sS", e_s_done)
                t_pre(base, base + 1024)
                for nch in range(2):
                    nc.tensor.matmul(ps[:, base + nch * 512:base + nch * 512 + 512],
                                     onesrow[0:1, :],
                                     boutrow[0:1, nch * 512:(nch + 1) * 512],
                                     start=True, stop=False)
                for cc in range(DS):
                    for nch in range(2):
                        c0 = base + nch * 512
                        ins = nc.tensor.matmul(
                            ps[:, c0:c0 + 512],
                            arena_k[:, cc * T + j * 128: cc * T + (j + 1) * 128],
                            wbufB[:, cc * D + nch * 512: cc * D + nch * 512 + 512],
                            start=False, stop=(cc == DS - 1))
                inc(ins, "sT")
                t_post(base, base + 1024)
                at = anew[:, par * D:(par + 1) * D]
                wtt("V", a_ld[j])
                wtt("V", anew_rd[par])
                wt("V", "sS", s_happly[par])
                r_pre("V", base, base + 1024)
                inc(nc.vector.tensor_tensor(at, ps[:, base:base + 1024],
                                            aold[:, par * D:(par + 1) * D],
                                            ALU.add), "sV")
                r_post("sV", base, base + 1024)
                sv_aold_done[par] = cnt["sV"]
                if j + 2 < TT and not last:
                    a_ld[j + 2] = dma("sq", aold[:, par * D:(par + 1) * D],
                                      a_hbm[(j + 2) * 128:(j + 3) * 128, :],
                                      waits=(g_aw[j + 2],
                                             ("sV", sv_aold_done[par])))
                if not last:
                    g_aw[j] = dma("gq", a_hbm[j * 128:(j + 1) * 128, :], at,
                                  waits=(("sV", cnt["sV"]),))
                    anew_rd[par] = g_aw[j]
                    r2, nm2 = ln_stats(at, "f_%d" % par)
                    hst = hstage[:, par * D:(par + 1) * D]
                    wtt("S", hst_rd[par])
                    wt("S", "sV", cnt["sV"])
                    inc(nc.scalar.activation(hst, at, AF.Identity,
                                             bias=nm2, scale=r2), "sS")
                    s_happly[par] = cnt["sS"]
                else:
                    r2, nm2 = ln_stats(at, "f_%d" % par)
                    inc(nc.vector.tensor_scalar(eq[:, 0:D], at, r2, nm2,
                                                ALU.mult, ALU.add), "sV")
                    wtt("V", x1_ld[j])
                    inc(nc.vector.tensor_tensor(at, eq[:, 0:D],
                                                ktmp[:, par * D:(par + 1) * D],
                                                ALU.add), "sV")
                    sv_x0_done[par] = cnt["sV"]
                    if j + 2 < TT:
                        x1_ld[j + 2] = dma("sq", ktmp[:, par * D:(par + 1) * D],
                                           x_in[(j + 2) * 128:(j + 3) * 128, :],
                                           waits=(("sV", sv_x0_done[par]),))
                    g_aw[j] = dma("gq", a_hbm[j * 128:(j + 1) * 128, :], at,
                                  waits=(("sV", cnt["sV"]),))
                    anew_rd[par] = g_aw[j]
                    r4, nm4 = ln_stats(at, "ffn_%d" % par)
                    fq = aold[:, par * D:(par + 1) * D]
                    inc(nc.vector.tensor_scalar(fq, at, r4, nm4,
                                                ALU.mult, ALU.add), "sV")
                    am = scol("xq_am_%d" % par)
                    iv = scol("xq_iv_%d" % par)
                    wt("V", "sS", xq_iv_free[par])
                    inc(nc.vector.reduce_max(am, fq, axis=AX.X,
                                             apply_absolute_value=True), "sV")
                    wt("V", "sV", cnt["sV"])
                    inc(nc.vector.tensor_single_scalar(am, am, 1e-8, ALU.max), "sV")
                    inc(nc.vector.reciprocal(iv, am), "sV")
                    wt("V", "sV", cnt["sV"])
                    inc(nc.vector.tensor_scalar(xs[:, j:j + 1], am, 1.0 / 127.0,
                                                None, ALU.mult), "sV")
                    xs_done[j] = cnt["sV"]
                    inc(nc.vector.tensor_scalar(iv, iv, 127.0, None, ALU.mult), "sV")
                    wt("V", "sV", cnt["sV"])
                    wt("S", "sV", cnt["sV"])
                    wtt("S", hst_rd[par])
                    fq_free = quant_cast(fq, hstage[:, par * D:(par + 1) * D],
                                         iv, par)
                    xq_iv_free[par] = fq_free
                    tok = dma("gq", xq_hbm[j * 128:(j + 1) * 128, :],
                              hstage[:, par * D:(par + 1) * D],
                              waits=(("sS", cnt["sS"]),))
                    hst_rd[par] = tok
                    xq_toks.append(tok)
                    if j + 2 < TT:
                        a_ld[j + 2] = dma("sq", aold[:, par * D:(par + 1) * D],
                                          a_hbm[(j + 2) * 128:(j + 3) * 128, :],
                                          waits=(g_aw[j + 2],
                                                 ("sS", fq_free)))
                barrier(2)
            if not last:
                for jj in (TT - 2, TT - 1):
                    hst_free, h_evac_done = h_transpose(
                        jj, jj % 2, s_happly[jj % 2])
                    hst_rd[jj % 2] = ("sT", hst_free)
            t_endF = cnt["sT"]
            barrier()

            if l + 1 < NL:
                wk_ld[l + 1] = load_w_part(wbufB, wqkv[l + 1, :, D:2 * D],
                                           waits=(("sT", t_endF),))

            t_endE_prev = t_endE
            t_endF_prev = t_endF
            wv_folded_prev = wv_folded
            if stop == ("L%d" % l):
                dbg_dump()
                return nc

        f_v_done = cnt["sV"]
        v_fin_free = cnt["sV"]
        wt("Y", "sT", t_endB3)
        xqT_w = [[] for _ in range(4)]
        for w in range(4):
            for s in range(DS):
                xqT_w[w].append(dma("sq",
                                    arena_h[:, s * T + w * 512: s * T + w * 512 + 512],
                                    xq_hbm[w * 512:(w + 1) * 512, s * 128:(s + 1) * 128],
                                    waits=xq_toks[4 * w:4 * w + 4], transpose=True))
        wqT_toks = []
        for s in range(DS):
            wqT_toks.append(dma("sq", wbufA[:, s * D:(s + 1) * D],
                                wq_hbm[:, s * 128:(s + 1) * 128],
                                waits=wqs_toks, transpose=True))
        wsb = eq[:, 0:D]
        fbb = eq[:, D:2 * D]
        wsb_tok = dma("sq", wsb, ws_hbm[None, :].partition_broadcast(128),
                      waits=list(wss_toks) + [("sV", v_fin_free)])
        fbb_tok = dma("sq", fbb, fb[None, :].partition_broadcast(128),
                      waits=[("sV", v_fin_free)])

        out_rd = [None, None]
        y_toks = []
        x1b_ld = [None] * TT
        for j in range(2):
            x1b_ld[j] = dma("sq", ktmp[:, (j % 2) * D:(j % 2 + 1) * D],
                            a_hbm[j * 128:(j + 1) * 128, :],
                            waits=(g_aw[j], ("sV", f_v_done)))
        sv_x1b_done = [0, 0]
        sv_fin_done = [0, 0]
        for j in range(TT):
            base = (j % 4) * 1024
            par = j % 2
            wtl("T", xqT_w[j // 4])
            wtl("T", wqT_toks)
            t_pre(base, base + 1024)
            for d in range(DS):
                for nch in range(2):
                    c0 = base + nch * 512
                    ins = nc.tensor.matmul(
                        ps[:, c0:c0 + 512],
                        arena_h[:, d * T + j * 128: d * T + (j + 1) * 128],
                        wbufA[:, d * D + nch * 512: d * D + nch * 512 + 512],
                        start=(d == 0), stop=(d == DS - 1))
            inc(ins, "sT")
            t_post(base, base + 1024)
            ft = anew[:, par * D:(par + 1) * D]
            wt("S", "sV", max(v_fin_free, xs_done[j], sv_fin_done[par]))
            r_pre("S", base, base + 1024)
            inc(nc.scalar.activation(ft, ps[:, base:base + 1024], AF.Copy,
                                     bias=0.0, scale=xs[:, j:j + 1]), "sS")
            r_post("sS", base, base + 1024)
            wt("V", "sS", cnt["sS"])
            wtt("V", wsb_tok)
            wtt("V", fbb_tok)
            inc(nc.vector.tensor_tensor(ft, ft, wsb, ALU.mult), "sV")
            inc(nc.vector.tensor_tensor(ft, ft, fbb, ALU.add), "sV")
            r5, nm5 = ln_stats(ft, "fo_%d" % par)
            inc(nc.vector.tensor_scalar(cbuf[:, :], ft, r5, nm5,
                                        ALU.mult, ALU.add), "sV")
            sv_fin_done[par] = cnt["sV"]
            ot = aold[:, par * D:(par + 1) * D]
            wtt("V", x1b_ld[j])
            wtt("V", out_rd[par])
            inc(nc.vector.tensor_tensor(ot, cbuf[:, :],
                                        ktmp[:, par * D:(par + 1) * D],
                                        ALU.add), "sV")
            sv_x1b_done[par] = cnt["sV"]
            if j + 2 < TT:
                x1b_ld[j + 2] = dma("sq", ktmp[:, par * D:(par + 1) * D],
                                    a_hbm[(j + 2) * 128:(j + 3) * 128, :],
                                    waits=(("sV", sv_x1b_done[par]),))
            tok = dma("gq", y_out[j * 128:(j + 1) * 128, :], ot,
                      waits=(("sV", cnt["sV"]),))
            out_rd[par] = tok
            y_toks.append(tok)
            barrier(2)

        if stop == "ffn":
            for sk in cnt:
                wt("G", sk, cnt[sk])
            for sk, v in quse.items():
                wt("G", sk, v)
            dma("gq", qdbg.rearrange("a b -> (a b)").rearrange("(a b) -> a b", a=T),
                xq_hbm)
            dma("gq", ydbg[0:D, :], wq_hbm)
            dma("gq", sdbg[:, 0:8], ws_hbm.rearrange("(p s) -> p s", p=128))
            for sk, v in quse.items():
                wt("G", sk, v)

        wtl("G", y_toks)

    return nc


_NC_CACHE = None


def kernel(L0_attn_x, L0_attn_w_qkv, L0_attn_w_out, L0_attn_b_out,
           L0_attn_gamma, L0_attn_beta, L0_ffn_weight, L0_ffn_bias):
    global _NC_CACHE
    if _NC_CACHE is None:
        _NC_CACHE = build_nc()
    nc = _NC_CACHE
    shared = {
        "wqkv": np.ascontiguousarray(L0_attn_w_qkv, dtype=np.float32),
        "wout": np.ascontiguousarray(L0_attn_w_out, dtype=np.float32),
        "bout": np.ascontiguousarray(L0_attn_b_out, dtype=np.float32),
        "gamma": np.ascontiguousarray(L0_attn_gamma, dtype=np.float32),
        "beta": np.ascontiguousarray(L0_attn_beta, dtype=np.float32),
        "fw": np.ascontiguousarray(L0_ffn_weight, dtype=np.float32),
        "fb": np.ascontiguousarray(L0_ffn_bias, dtype=np.float32),
    }
    x = np.ascontiguousarray(L0_attn_x, dtype=np.float32)
    in_maps = [dict(shared, x=x[b]) for b in range(8)]
    res = run_bass_kernel_spmd(nc, in_maps, core_ids=list(range(8)))
    return np.stack([np.asarray(r["y"], dtype=np.float32)
                     for r in res.results], axis=0)
